# version 04 (final)
# baseline (speedup 1.0000x reference)
import numpy as np
import concourse.bass as bass
import concourse.mybir as mybir
from concourse.bass_utils import run_bass_kernel_spmd

F32 = mybir.dt.float32
BF16 = mybir.dt.bfloat16
U16 = mybir.dt.uint16
AF = mybir.ActivationFunctionType
ALU = mybir.AluOpType
AX = mybir.AxisListType

T = 1024
H = 4096
D = 1024
NTC = 8
NHC = 32
NDC = 8
NKC = 16
EPS = 1e-5
QSCALE = 0.03125

KB = 1024
OFF_A = 0
OFF_PT = 0
OFF_K = 32 * KB
OFF_WQ0 = 64 * KB
OFF_WQ1 = 72 * KB
OFF_KVST0 = 80 * KB
OFF_KVST1 = 82 * KB
OFF_VTM = 84 * KB
OFF_WO = 64 * KB
OFF_VFFN = 64 * KB
OFF_XDB = 128 * KB
OFF_VSB = 128 * KB
OFF_X1DB = 128 * KB
OFF_BFFN = 144 * KB
OFF_QSB = 160 * KB
OFF_WF0 = 160 * KB
OFF_SLOT2 = 176 * KB
OFF_WF1 = 176 * KB
OFF_SLOT3 = 184 * KB
OFF_BO = 192 * KB
OFF_GATE = 200 * KB
OFF_ONES = OFF_GATE + 512
OFF_BQKV = OFF_ONES + 256
OFF_RPAD = OFF_BQKV + 128
OFF_RTT = OFF_RPAD + 2048
OFF_OCSL = OFF_RTT + 2048
OFF_STATS = OFF_OCSL + 1024
OFF_SPART = OFF_STATS + 128
OFF_SQPART = OFF_SPART + 32
OFF_FSUM = OFF_SQPART + 32
OFF_FSQ = OFF_FSUM + 512
OFF_OSC = OFF_FSQ + 512
OFF_END = OFF_OSC + 512
assert OFF_END <= 212863, OFF_END
SBUF_BASE = 16640
assert SBUF_BASE + OFF_END <= 229376 - 129, SBUF_BASE + OFF_END


class Sem:
    def __init__(self, nc, name):
        self.h = nc.alloc_semaphore(name)
        self.n = 0

    def bump(self, k=1):
        self.n += k
        return self.n


def build_program(debug=False):
    light = (debug == 2)
    if light:
        debug = False
    nc = bass.Bass("TRN2", target_bir_lowering=False)

    x_tm = nc.dram_tensor("x_tm", [T, H], F32, kind="ExternalInput").ap()
    wqkv_t = nc.dram_tensor("wqkv_t", [24, H, 128], F32, kind="ExternalInput").ap()
    bqkv_r = nc.dram_tensor("bqkv_r", [128, 24], F32, kind="ExternalInput").ap()
    wo_r = nc.dram_tensor("wo_r", [D, H], F32, kind="ExternalInput").ap()
    bo_row = nc.dram_tensor("bo_row", [1, H], F32, kind="ExternalInput").ap()
    wffn_t = nc.dram_tensor("wffn_t", [16, H, 256], F32, kind="ExternalInput").ap()
    bffn_bcast = nc.dram_tensor("bffn_bcast", [128, H], F32, kind="ExternalInput").ap()
    gate_t = nc.dram_tensor("gate_t", [H, 8], F32, kind="ExternalInput").ap()
    icsc_d = nc.dram_tensor("icsc_d", [8, H], F32, kind="ExternalInput").ap()
    ocsc_d = nc.dram_tensor("ocsc_d", [8, H], F32, kind="ExternalInput").ap()

    kv_out_d = nc.dram_tensor("kv_out_d", [2048, 1024], BF16).ap()
    kv_in_d = nc.dram_tensor("kv_in_d", [4096, 1024], BF16).ap()
    x1_kind = "ExternalOutput" if (debug or light) else "Internal"
    if light:
        lt_att = nc.dram_tensor("lt_att", [128, 8 * 1024], F32, kind="ExternalOutput").ap()
        lt_q = nc.dram_tensor("lt_q", [128, 8 * 1024], F32, kind="ExternalOutput").ap()
        lt_pt = nc.dram_tensor("lt_pt", [128, 16 * 1024], F32, kind="ExternalOutput").ap()

    x1_d = nc.dram_tensor("x1_d", [T, H], F32, kind=x1_kind).ap()
    y_d = nc.dram_tensor("y_d", [T, H], F32, kind="ExternalOutput").ap()
    if debug:
        dbg_xnT = nc.dram_tensor("dbg_xnT", [128, 32 * 1024], F32, kind="ExternalOutput").ap()
        dbg_kvo = nc.dram_tensor("dbg_kvo", [2048, 1024], F32, kind="ExternalOutput").ap()
        dbg_q = nc.dram_tensor("dbg_q", [128, 8 * 1024], F32, kind="ExternalOutput").ap()
        dbg_pt = nc.dram_tensor("dbg_pt", [128, 16 * 1024], F32, kind="ExternalOutput").ap()
        dbg_att = nc.dram_tensor("dbg_att", [128, 8 * 1024], F32, kind="ExternalOutput").ap()
        dbg_xsT = nc.dram_tensor("dbg_xsT", [128, 32 * 1024], F32, kind="ExternalOutput").ap()
        dbg_rt = nc.dram_tensor("dbg_rt", [128, 1024], F32, kind="ExternalOutput").ap()
        dbg_kv = nc.dram_tensor("dbg_kv", [4096, 1024], F32, kind="ExternalOutput").ap()

    manual_names = set()

    def sbt(name, shape, dtype, off):
        manual_names.add(name)
        return nc.alloc_sbuf_tensor_at(name, shape, dtype, offset=SBUF_BASE + off).ap()

    xnT = sbt("xnT", [128, 32 * 1024], BF16, OFF_A)
    pt_sb = sbt("pt_sb", [128, 16 * 1024], BF16, OFF_PT)
    K_sb = sbt("K_sb", [128, 8 * 2048], BF16, OFF_K)
    x1nT = sbt("x1nT", [128, 32 * 1024], BF16, OFF_A)

    wq_db = [sbt("wq0", [128, 32 * 128], BF16, OFF_WQ0),
             sbt("wq1", [128, 32 * 128], BF16, OFF_WQ1)]
    kv_st = [sbt("kvst0", [128, 1024], BF16, OFF_KVST0),
             sbt("kvst1", [128, 1024], BF16, OFF_KVST1)]
    vtm_sb = sbt("vtm_sb", [128, 8 * 1024], BF16, OFF_VTM)
    wo_sb = sbt("wo_sb", [128, 8 * 4096], BF16, OFF_WO)
    vffn = sbt("vffn", [128, 8 * 4096], BF16, OFF_VFFN)

    x_db = [sbt("xdb0", [128, 4096], F32, OFF_XDB),
            sbt("xdb1", [128, 4096], F32, OFF_XDB + 16 * KB)]
    V_sb = sbt("V_sb", [128, 16 * 1024], BF16, OFF_VSB)
    x1_db = [sbt("x1db0", [128, 2048], F32, OFF_X1DB),
             sbt("x1db1", [128, 2048], F32, OFF_X1DB + 8 * KB)]
    bffn_bc = sbt("bffn_bc", [128, 4096], BF16, OFF_BFFN)

    q_sb = sbt("q_sb", [128, 8 * 1024], BF16, OFF_QSB)
    att_sb = sbt("att_sb", [128, 8 * 1024], BF16, OFF_QSB)
    wf_db = [sbt("wf0", [128, 32 * 256], BF16, OFF_WF0),
             sbt("wf1", [128, 32 * 256], BF16, OFF_WF1)]
    xn_chunk = sbt("xn_chunk", [128, 4096], BF16, OFF_SLOT2)
    exp_stage = sbt("exp_stage", [128, 2048], F32, OFF_SLOT2)
    vatt_chunk = sbt("vatt_chunk", [128, 4096], BF16, OFF_SLOT2)
    sq8k = sbt("sq8k", [128, 4096], BF16, OFF_SLOT3)
    p_chunk = sbt("p_chunk", [128, 2048], BF16, OFF_SLOT3)
    x1n_chunk = sbt("x1n_chunk", [128, 4096], BF16, OFF_SLOT3)

    bo_sb = sbt("bo_sb", [1, 4096], BF16, OFF_BO)
    icsc_sb = sbt("icsc_sb", [8, 4096], BF16, OFF_BO)
    gate_sb = sbt("gate_sb", [128, 256], BF16, OFF_GATE)
    ones_sb = sbt("ones_sb", [1, 128], BF16, OFF_ONES)
    bqkv_sb = sbt("bqkv_sb", [128, 24], F32, OFF_BQKV)
    rpad = sbt("rpad", [128, 1024], BF16, OFF_RPAD)
    scrap512 = sbt("scrap512", [128, 512], BF16, OFF_RPAD)
    scrap256a = sbt("scrap256a", [128, 256], BF16, OFF_RPAD)
    scrap256b = sbt("scrap256b", [128, 256], BF16, OFF_RPAD + 512)
    rtT = sbt("rtT", [128, 1024], BF16, OFF_RTT)
    ocsl = [sbt("ocsl0", [8, 256], BF16, OFF_OCSL),
            sbt("ocsl1", [8, 256], BF16, OFF_OCSL + 512)]
    st = sbt("st", [128, 32], F32, OFF_STATS)
    spart = sbt("spart", [128, 8], F32, OFF_SPART)
    sqpart = sbt("sqpart", [128, 8], F32, OFF_SQPART)
    fsum = sbt("fsum", [128, 128], F32, OFF_FSUM)
    fsq = sbt("fsq", [128, 128], F32, OFF_FSQ)
    osc_sb = sbt("osc_sb", [128, 256], BF16, OFF_OSC)

    SUM, SQ, M, T1, M2, VE, LN_, HALF, RSTD, RM, RSUM, RRS, SQB, RSUMB = range(14)

    def stc(i):
        return st[:, i:i + 1]

    bank = [nc.alloc_psum_tensor(f"pb{i}", [128, 512], F32) for i in range(8)]
    bank_gates = [[] for _ in range(8)]

    ops = {k: [] for k in "tvsgy"}

    def emit(eng, fn):
        ops[eng].append(fn)

    def wait(eng, s_, val):
        if val > 0:
            ops[eng].append(lambda e, _s=s_.h, _v=val: e.wait_ge(_s, _v))

    S = {}

    def sem(name):
        if name not in S:
            S[name] = Sem(nc, name)
        return S[name]

    def bank_acquire(b):
        for s_, v_ in bank_gates[b]:
            wait("t", s_, v_)
        bank_gates[b] = []

    def bank_release(b, s_, v_):
        bank_gates[b].append((s_, v_))

    def dinc(eng, s_, k=1):
        emit(eng, lambda e, _s=s_.h, _k=k: e.drain().then_inc(_s, _k))
        return s_.bump(k)

    def vdrain():
        emit("v", lambda e: e.drain())

    def emit_rstd():
        vsv = sem("vstat").n
        wait("s", sem("vstat"), vsv)
        emit("s", lambda e: e.activation(stc(LN_), stc(VE), AF.Ln))
        slv = dinc("s", sem("sln"))
        wait("v", sem("sln"), slv)
        emit("v", lambda e: e.tensor_scalar(stc(HALF), stc(LN_), -0.5, None, ALU.mult))
        vhv = dinc("v", sem("vhalf"))
        wait("s", sem("vhalf"), vhv)
        emit("s", lambda e: e.activation(stc(RSTD), stc(HALF), AF.Exp))
        return dinc("s", sem("srstd"))

    def emit_meanvar(sqc=SQ):
        vdrain()
        emit("v", lambda e: e.tensor_scalar(stc(M), stc(SUM), 1.0 / H, None, ALU.mult))
        emit("v", lambda e, _q=sqc: e.tensor_scalar(stc(T1), stc(_q), 1.0 / H, EPS, ALU.mult, ALU.add))
        vdrain()
        emit("v", lambda e: e.tensor_scalar(stc(M2), stc(M), stc(M), None, ALU.mult))
        vdrain()
        emit("v", lambda e: e.tensor_scalar(stc(VE), stc(T1), stc(M2), None, ALU.subtract))
        dinc("v", sem("vstat"))

    def load_act_set(e):
        from concourse.hw_specs import get_activation_tables
        tabs = get_activation_tables(nc.m.arch)
        idx = list(tabs.keys()).index("natural_log_exp_and_others")
        e.add_instruction(mybir.InstLoadActFuncSet(
            name=e.bass.get_next_instruction_name(),
            act_func_set_id=idx, ins=[], outs=[]))
    emit("s", load_act_set)
    emit("g", lambda e: e.dma_start(
        out=gate_sb[:, :].rearrange("p (hc e) -> p hc e", hc=32),
        in_=gate_t.rearrange("(hc p) e -> p hc e", p=128)).then_inc(sem("gld").h, 16))
    sem("gld").bump(16)
    emit("g", lambda e: e.dma_start(out=bqkv_sb[:, :], in_=bqkv_r).then_inc(sem("gld").h, 16))
    gld_all = sem("gld").bump(16)
    emit("g", lambda e: e.dma_start(out=bo_sb[:, :], in_=bo_row).then_inc(sem("boll").h, 16))
    sem("boll").bump(16)
    emit("g", lambda e: e.memset(ones_sb[:, :], 1.0))
    dinc("g", sem("ones"))

    for tc in range(NTC):
        buf = tc % 2
        if tc >= 2:
            wait("s", sem("xnc"), tc - 1)
        emit("s", lambda e, _t=tc, _b=buf: e.dma_start(
            out=x_db[_b][:, :], in_=x_tm[_t * 128:(_t + 1) * 128, :]).then_inc(sem("xld").h, 16))
        xval = sem("xld").bump(16)
        wait("v", sem("xld"), xval)
        if tc >= 1:
            wait("v", sem("tpx"), tc * 16)
        emit("v", lambda e, _b=buf: e.tensor_scalar(
            xn_chunk[:, :], x_db[_b][:, :], 1.0, 0.0, ALU.mult, ALU.add,
            accum_out=stc(SUM)))
        wait("s", sem("xld"), xval)
        emit("s", lambda e, _b=buf: e.activation(sq8k[:, :], x_db[_b][:, :], AF.Square))
        sqv = dinc("s", sem("ssq"))
        wait("v", sem("ssq"), sqv)
        emit("v", lambda e: e.tensor_scalar(
            sq8k[:, :], sq8k[:, :], 1.0, 0.0, ALU.mult, ALU.add, accum_out=stc(SQ)))
        emit_meanvar(SQ)
        srv = emit_rstd()
        wait("v", sem("srstd"), srv)
        emit("v", lambda e, _b=buf: e.tensor_scalar(
            xn_chunk[:, :], x_db[_b][:, :], stc(M), stc(RSTD), ALU.subtract,
            ALU.mult))
        dinc("v", sem("xnc"))
        wait("y", sem("xnc"), tc + 1)
        emit("y", lambda e, _t=tc: e.dma_start(
            out=xnT.rearrange("p (hc t) -> p hc t", hc=32)[:, :, _t * 128:(_t + 1) * 128],
            in_=xn_chunk[:, :], transpose=True).then_inc(sem("tpx").h, 16))
        sem("tpx").bump(16)

    if debug:
        wait("g", sem("tpx"), 8 * 16)
        emit("g", lambda e: e.dma_start(out=dbg_xnT, in_=xnT[:, :]).then_inc(sem("dbg").h, 16))
        sem("dbg").bump(16)

    def qev_val(i, th):
        return 2 * i + th + 1

    for i in range(24):
        buf = i % 2
        if i >= 2:
            wait("g", sem("qmm"), qev_val(i - 2, 1))
        emit("g", lambda e, _i=i, _b=buf: e.dma_start(
            out=wq_db[_b][:, :].rearrange("p (hc c) -> p hc c", hc=32),
            in_=wqkv_t[_i].rearrange("(hc p) c -> p hc c", p=128)).then_inc(sem("wql").h, 16))
        sem("wql").bump(16)
        if 2 <= i <= 9:
            j = i - 2
            wait("g", sem("qev"), qev_val(j, 1))
            emit("g", lambda e, _j=j: e.dma_start(
                out=kv_out_d[_j * 128:(_j + 1) * 128, :],
                in_=kv_st[_j % 2][:, :]).then_inc(sem("kvb").h, 16))
            sem("kvb").bump(16)

    wait("g", sem("vtp"), 8 * 16)
    emit("g", lambda e: e.dma_start(
        out=kv_out_d[1024:2048, :].rearrange("(tc p) d -> p tc d", p=128),
        in_=vtm_sb[:, :].rearrange("p (tc d) -> p tc d", tc=8)).then_inc(sem("kvb").h, 16))
    kvbv = sem("kvb").bump(16)
    wait("g", sem("kvb"), kvbv)
    wait("g", sem("qev"), 48)
    emit("g", lambda e: e.collective_compute(
        "AllGather", ALU.bypass,
        replica_groups=[[0, 1], [2, 3], [4, 5], [6, 7]],
        ins=[kv_out_d.opt()], outs=[kv_in_d.opt()]).then_inc(sem("ccs").h))
    sem("ccs").bump(1)

    for dc in range(8):
        wait("y", sem("qev"), qev_val(8 + dc, 1))
        emit("y", lambda e, _dc=dc: e.dma_start(
            out=vtm_sb.rearrange("p (tc d) -> p tc d", tc=8)[:, :, _dc * 128:(_dc + 1) * 128],
            in_=kv_st[_dc % 2][:, :], transpose=True).then_inc(sem("vtp").h, 16))
        sem("vtp").bump(16)

    wait("t", sem("tpx"), 8 * 16)
    wait("v", sem("gld"), gld_all)
    for i in range(24):
        buf = i % 2
        wait("t", sem("wql"), (i + 1) * 16)
        for th in range(2):
            b = 2 * (i % 2) + th + (4 if i >= 16 else 0)
            bank_acquire(b)
            for hc in range(NHC):
                emit("t", lambda e, _b=b, _buf=buf, _hc=hc, _th=th: e.matmul(
                    bank[_b][:, :],
                    wq_db[_buf][:, _hc * 128:(_hc + 1) * 128],
                    xnT[:, _hc * 1024 + _th * 512: _hc * 1024 + (_th + 1) * 512],
                    start=(_hc == 0), stop=(_hc == NHC - 1)))
            qv = dinc("t", sem("qmm"))
            wait("v", sem("qmm"), qv)
            if th == 0 and i >= 2 and i < 16:
                if i - 2 < 8:
                    wait("v", sem("kvb"), 16 * (i - 1))
                else:
                    wait("v", sem("vtp"), 16 * (i - 9))
            if i < 16:
                emit("v", lambda e, _i=i, _th=th, _b=b: e.tensor_scalar(
                    kv_st[_i % 2][:, _th * 512:(_th + 1) * 512], bank[_b][:, :],
                    bqkv_sb[:, _i:_i + 1], None, ALU.add))
            else:
                qc_col = i - 16
                emit("v", lambda e, _i=i, _q=qc_col, _th=th, _b=b: e.tensor_scalar(
                    q_sb[:, _q * 1024 + _th * 512: _q * 1024 + (_th + 1) * 512],
                    bank[_b][:, :], bqkv_sb[:, _i:_i + 1], QSCALE, ALU.add,
                    ALU.mult))
            qe = dinc("v", sem("qev"))
            assert qe == qev_val(i, th)
            bank_release(b, sem("qev"), qe)

    wait("y", sem("ccs"), 1)
    wait("y", sem("qev"), 48)
    emit("y", lambda e: e.dma_start(
        out=V_sb[:, 0:8 * 1024].rearrange("p (kc d) -> p kc d", kc=8),
        in_=kv_in_d[1024:2048, :].rearrange("(kc p) d -> p kc d", p=128)).then_inc(sem("vld").h, 16))
    sem("vld").bump(16)
    emit("y", lambda e: e.dma_start(
        out=V_sb[:, 8 * 1024:].rearrange("p (kc d) -> p kc d", kc=8),
        in_=kv_in_d[3072:4096, :].rearrange("(kc p) d -> p kc d", p=128)).then_inc(sem("vld").h, 16))
    sem("vld").bump(16)
    wait("y", sem("qmm"), 48)
    emit("y", lambda e: e.dma_start(
        out=K_sb.rearrange("p (dc t) -> p dc t", dc=8)[:, :, 0:1024],
        in_=kv_in_d[0:1024, :].rearrange("(dc p) t -> p dc t", p=128)).then_inc(sem("kld").h, 16))
    sem("kld").bump(16)
    emit("y", lambda e: e.dma_start(
        out=K_sb.rearrange("p (dc t) -> p dc t", dc=8)[:, :, 1024:2048],
        in_=kv_in_d[2048:3072, :].rearrange("(dc p) t -> p dc t", p=128)).then_inc(sem("kld").h, 16))
    sem("kld").bump(16)
    if debug:
        wait("g", sem("ccs"), 1)
        emit("g", lambda e: e.dma_start(out=dbg_kvo, in_=kv_out_d).then_inc(sem("dbg").h, 16))
        sem("dbg").bump(16)
        emit("g", lambda e: e.dma_start(out=dbg_kv, in_=kv_in_d).then_inc(sem("dbg").h, 16))
        sem("dbg").bump(16)

    wait("g", sem("qmm"), 48)
    wait("g", sem("qev"), 48)
    emit("g", lambda e: e.dma_start(
        out=wo_sb[:, :].rearrange("p (dc n) -> p dc n", dc=8),
        in_=wo_r.rearrange("(dc p) n -> p dc n", p=128)).then_inc(sem("wol").h, 16))
    sem("wol").bump(16)

    wait("t", sem("kld"), 32)
    wait("t", sem("qev"), 48)
    wait("t", sem("wol"), 16)
    for qc in range(NTC):
        for dc in range(NDC):
            for ks in range(4):
                b = 4 + ks
                if dc == 0:
                    bank_acquire(b)
                emit("t", lambda e, _b=b, _dc=dc, _qc=qc, _ks=ks: e.matmul(
                    bank[_b][:, :],
                    q_sb[:, _dc * 1024 + _qc * 128: _dc * 1024 + (_qc + 1) * 128],
                    K_sb[:, _dc * 2048 + _ks * 512: _dc * 2048 + (_ks + 1) * 512],
                    start=(_dc == 0), stop=(_dc == NDC - 1)))
        sv = dinc("t", sem("smm"))
        wait("v", sem("smm"), sv)
        if qc >= 1:
            wait("v", sem("pexp"), qc)
        for ks in range(4):
            emit("v", lambda e, _ks=ks: e.reduce_max(
                spart[:, _ks:_ks + 1], bank[4 + _ks][:, :], axis=AX.X))
        vdrain()
        emit("v", lambda e: e.reduce_max(stc(RM), spart[:, 0:4], axis=AX.X))
        vdrain()
        for ks in range(4):
            emit("v", lambda e, _ks=ks: e.tensor_scalar(
                exp_stage[:, _ks * 512:(_ks + 1) * 512], bank[4 + _ks][:, :],
                stc(RM), None, ALU.subtract))
            sevv = dinc("v", sem("sev"))
            bank_release(4 + ks, sem("sev"), sevv)
        wait("s", sem("sev"), sem("sev").n)
        if qc >= 1:
            wait("s", sem("ptp"), qc * 16)
        emit("s", lambda e: e.activation(p_chunk[:, :], exp_stage[:, :], AF.Exp))
        pev = dinc("s", sem("pexp"))
        wait("v", sem("pexp"), pev)
        emit("v", lambda e: e.tensor_scalar(
            p_chunk[:, :], p_chunk[:, :], 1.0, 0.0, ALU.mult, ALU.add,
            accum_out=stc(RSUM)))
        vdrain()
        emit("v", lambda e: e.reciprocal(stc(RRS), stc(RSUM)))
        vdrain()
        emit("v", lambda e: e.tensor_scalar(
            p_chunk[:, :], p_chunk[:, :], stc(RRS), None, ALU.mult))
        dinc("v", sem("pdone"))
        wait("y", sem("pdone"), qc + 1)
        emit("y", lambda e, _qc=qc: e.dma_start(
            out=pt_sb.rearrange("p (kc q) -> p kc q", kc=16)[:, :, _qc * 128:(_qc + 1) * 128],
            in_=p_chunk[:, :], transpose=True).then_inc(sem("ptp").h, 16))
        sem("ptp").bump(16)

    if debug:
        wait("g", sem("ptp"), 8 * 16)
        emit("g", lambda e: e.dma_start(out=dbg_q, in_=q_sb[:, :]).then_inc(sem("dbg").h, 16))
        sem("dbg").bump(16)
        emit("g", lambda e: e.dma_start(out=dbg_pt, in_=pt_sb[:, :]).then_inc(sem("dbg").h, 16))
        dbgv = sem("dbg").bump(16)
        wait("t", sem("dbg"), dbgv)
    wait("t", sem("ptp"), 8 * 16)
    wait("t", sem("vld"), 32)
    for dc in range(NDC):
        for qs in range(2):
            b = 2 * (dc % 2) + qs
            bank_acquire(b)
            for kc in range(NKC):
                emit("t", lambda e, _b=b, _dc=dc, _qs=qs, _kc=kc: e.matmul(
                    bank[_b][:, :],
                    V_sb[:, _kc * 1024 + _dc * 128: _kc * 1024 + (_dc + 1) * 128],
                    pt_sb[:, _kc * 1024 + _qs * 512: _kc * 1024 + (_qs + 1) * 512],
                    start=(_kc == 0), stop=(_kc == NKC - 1)))
            av = dinc("t", sem("avmm"))
            wait("v", sem("avmm"), av)
            emit("v", lambda e, _b=b, _dc=dc, _qs=qs: e.tensor_copy(
                att_sb[:, _dc * 1024 + _qs * 512: _dc * 1024 + (_qs + 1) * 512],
                bank[_b][:, :]))
            avev = dinc("v", sem("avev"))
            bank_release(b, sem("avev"), avev)
    if debug:
        wait("g", sem("avev"), 16)
        emit("g", lambda e: e.dma_start(out=dbg_att, in_=att_sb[:, :]).then_inc(sem("dbg").h, 16))
        sem("dbg").bump(16)

    if light:
        wait("g", sem("kld"), 32)
        emit("g", lambda e: e.dma_start(out=lt_q, in_=q_sb[:, :]).then_inc(sem("dbg").h, 16))
        sem("dbg").bump(16)
        wait("g", sem("ptp"), 8 * 16)
        emit("g", lambda e: e.dma_start(out=lt_pt, in_=pt_sb[:, :]).then_inc(sem("dbg").h, 16))
        sem("dbg").bump(16)
        wait("g", sem("avev"), 16)
        emit("g", lambda e: e.dma_start(out=lt_att, in_=att_sb[:, :]).then_inc(sem("dbg").h, 16))
        sem("dbg").bump(16)

    wait("t", sem("wol"), 16)
    wait("t", sem("boll"), 16)
    wait("t", sem("ones"), 1)
    wait("t", sem("avev"), 16)
    for tc in range(NTC):
        buf = tc % 2
        if tc < 2:
            wait("s", sem("avmm"), 16)
        else:
            wait("s", sem("x1st"), (tc - 1) * 16)
            wait("s", sem("x1nc"), tc - 1)
        emit("s", lambda e, _t=tc, _b=buf: e.dma_start(
            out=x_db[_b][:, :], in_=x_tm[_t * 128:(_t + 1) * 128, :]).then_inc(sem("xld").h, 16))
        xval = sem("xld").bump(16)

        osq_base = sem("osq").n
        for cp in range(8):
            b = (tc * 8 + cp) % 8
            bank_acquire(b)
            for dc in range(NDC):
                emit("t", lambda e, _b=b, _dc=dc, _tc=tc, _cp=cp: e.matmul(
                    bank[_b][:, :],
                    att_sb[:, _dc * 1024 + _tc * 128: _dc * 1024 + (_tc + 1) * 128],
                    wo_sb[:, _dc * 4096 + _cp * 512: _dc * 4096 + (_cp + 1) * 512],
                    start=(_dc == 0), stop=False))
            emit("t", lambda e, _b=b, _cp=cp: e.matmul(
                bank[_b][:, :], ones_sb[0:1, :], bo_sb[0:1, _cp * 512:(_cp + 1) * 512],
                start=False, stop=True))
            ov = dinc("t", sem("omm"))
            wait("v", sem("omm"), ov)
            emit("v", lambda e, _b=b, _cp=cp: e.tensor_scalar(
                vatt_chunk[:, _cp * 512:(_cp + 1) * 512], bank[_b][:, :], 1.0, 0.0,
                ALU.mult, ALU.add, accum_out=spart[:, _cp:_cp + 1]))
            emit("v", lambda e, _cp=cp: e.tensor_tensor(
                scrap512[:, :], vatt_chunk[:, _cp * 512:(_cp + 1) * 512],
                vatt_chunk[:, _cp * 512:(_cp + 1) * 512], ALU.mult))
            emit("v", lambda e, _cp=cp: e.tensor_scalar(
                scrap512[:, :], scrap512[:, :], 1.0, 0.0, ALU.mult, ALU.add,
                accum_out=sqpart[:, _cp:_cp + 1]))
            oevv = dinc("v", sem("oev"))
            bank_release(b, sem("oev"), oevv)
        emit("v", lambda e: e.reduce_sum(stc(SUM), spart[:, 0:8], axis=AX.X))
        emit("v", lambda e: e.reduce_sum(stc(SQ), sqpart[:, 0:8], axis=AX.X))
        emit_meanvar()
        srv = emit_rstd()
        wait("v", sem("srstd"), srv)
        emit("v", lambda e: e.tensor_scalar(
            vatt_chunk[:, :], vatt_chunk[:, :], stc(M), stc(RSTD), ALU.subtract, ALU.mult))
        vdrain()
        wait("v", sem("xld"), xval)
        emit("v", lambda e, _b=buf: e.tensor_tensor(
            x_db[_b][:, :], x_db[_b][:, :], vatt_chunk[:, :], ALU.add))
        x1rv = dinc("v", sem("x1rdy"))
        emit("v", lambda e, _b=buf: e.tensor_scalar(
            vatt_chunk[:, :], x_db[_b][:, :], 1.0, 0.0, ALU.mult, ALU.add,
            accum_out=stc(SUM)))
        wait("y", sem("x1rdy"), x1rv)
        emit("y", lambda e, _t=tc, _b=buf: e.dma_start(
            out=x1_d[_t * 128:(_t + 1) * 128, :], in_=x_db[_b][:, :]).then_inc(sem("x1st").h, 16))
        sem("x1st").bump(16)
        wait("s", sem("x1rdy"), x1rv)
        if tc >= 1:
            wait("s", sem("tp2"), tc * 16)
        emit("s", lambda e, _b=buf: e.activation(x1n_chunk[:, :], x_db[_b][:, :], AF.Square))
        sqv = dinc("s", sem("ssq"))
        wait("v", sem("ssq"), sqv)
        emit("v", lambda e: e.tensor_scalar(
            x1n_chunk[:, :], x1n_chunk[:, :], 1.0, 0.0, ALU.mult, ALU.add,
            accum_out=stc(SQ)))
        emit_meanvar()
        srv = emit_rstd()
        wait("v", sem("srstd"), srv)
        emit("v", lambda e, _b=buf: e.tensor_scalar(
            x1n_chunk[:, :], x_db[_b][:, :], stc(M), stc(RSTD), ALU.subtract,
            ALU.mult))
        dinc("v", sem("x1nc"))
        wait("y", sem("x1nc"), tc + 1)
        emit("y", lambda e, _t=tc: e.dma_start(
            out=x1nT.rearrange("p (hc t) -> p hc t", hc=32)[:, :, _t * 128:(_t + 1) * 128],
            in_=x1n_chunk[:, :], transpose=True).then_inc(sem("tp2").h, 16))
        sem("tp2").bump(16)

    wait("g", sem("omm"), 64)
    emit("g", lambda e: e.dma_start(out=icsc_sb[:, :], in_=icsc_d).then_inc(sem("icl").h, 16))
    sem("icl").bump(16)
    wait("g", sem("x1st"), 128)
    wait("g", sem("x1nc"), 8)
    emit("g", lambda e: e.dma_start(out=bffn_bc[:, :], in_=bffn_bcast).then_inc(sem("bfl").h, 16))
    sem("bfl").bump(16)

    wait("t", sem("tp2"), 8 * 16)
    wait("t", sem("gld"), gld_all)
    for tc in range(NTC):
        bank_acquire(4)
        for hc in range(NHC):
            emit("t", lambda e, _hc=hc, _tc=tc: e.matmul(
                bank[4][:, 0:8],
                x1nT[:, _hc * 1024 + _tc * 128: _hc * 1024 + (_tc + 1) * 128],
                gate_sb[:, _hc * 8:(_hc + 1) * 8],
                start=(_hc == 0), stop=(_hc == NHC - 1)))
        rv = dinc("t", sem("rmm"))
        wait("v", sem("rmm"), rv)
        emit("v", lambda e: e.reduce_max(stc(RM), bank[4][:, 0:8], axis=AX.X))
        vdrain()
        emit("v", lambda e: e.tensor_scalar(
            st[:, 16:24], bank[4][:, 0:8], stc(RM), None, ALU.subtract))
        revv = dinc("v", sem("rev"))
        bank_release(4, sem("rev"), revv)
        wait("s", sem("rev"), revv)
        emit("s", lambda e, _tc=tc: e.activation(
            rpad[:, _tc * 128:_tc * 128 + 8], st[:, 16:24], AF.Exp))
        rev2 = dinc("s", sem("rexp"))
        wait("v", sem("rexp"), rev2)
        emit("v", lambda e, _tc=tc: e.tensor_scalar(
            rpad[:, _tc * 128:_tc * 128 + 8], rpad[:, _tc * 128:_tc * 128 + 8],
            1.0, 0.0, ALU.mult, ALU.add, accum_out=stc(RSUM)))
        vdrain()
        emit("v", lambda e: e.reciprocal(stc(RRS), stc(RSUM)))
        vdrain()
        emit("v", lambda e, _tc=tc: e.tensor_scalar(
            rpad[:, _tc * 128:_tc * 128 + 8], rpad[:, _tc * 128:_tc * 128 + 8],
            stc(RRS), None, ALU.mult))
        dinc("v", sem("rnrm"))
    wait("y", sem("rnrm"), sem("rnrm").n)
    emit("y", lambda e: e.dma_start(
        out=rtT.rearrange("p (tc q) -> p tc q", tc=8), in_=rpad[:, :],
        transpose=True).then_inc(sem("rtp").h, 16))
    sem("rtp").bump(16)
    if debug:
        wait("g", sem("rtp"), 16)
        emit("g", lambda e: e.dma_start(out=dbg_rt, in_=rtT[:, :]).then_inc(sem("dbg").h, 16))
        sem("dbg").bump(16)

    wait("t", sem("rtp"), 16)
    wait("t", sem("icl"), 16)
    for fc in range(NHC):
        for qs in range(2):
            b = 2 * (fc % 2) + qs
            bank_acquire(b)
            emit("t", lambda e, _b=b, _fc=fc, _qs=qs: e.matmul(
                bank[_b][:, :], icsc_sb[0:8, _fc * 128:(_fc + 1) * 128],
                rtT[0:8, _qs * 512:(_qs + 1) * 512],
                start=True, stop=True))
            imv = dinc("t", sem("imm"))
            wait("v", sem("imm"), imv)
            emit("v", lambda e, _b=b, _fc=fc, _qs=qs: e.tensor_tensor(
                x1nT[:, _fc * 1024 + _qs * 512: _fc * 1024 + (_qs + 1) * 512],
                bank[_b][:, :],
                x1nT[:, _fc * 1024 + _qs * 512: _fc * 1024 + (_qs + 1) * 512],
                ALU.mult))
            ievv = dinc("v", sem("iev"))
            bank_release(b, sem("iev"), ievv)
    iev_all = sem("iev").n
    if debug:
        wait("g", sem("iev"), iev_all)
        emit("g", lambda e: e.dma_start(out=dbg_xsT, in_=x1nT[:, :]).then_inc(sem("dbg").h, 16))
        sem("dbg").bump(16)

    wait("v", sem("bfl"), 16)
    emit("v", lambda e: e.reduce_sum(st[:, 24:25], bffn_bc[:, :], axis=AX.X))
    dinc("v", sem("btotv"))

    wfl_vals = []
    ocl_vals = []

    def emit_wf_load(op):
        buf = op % 2
        if op == 0:
            wait("g", sem("omm"), 64)
        elif op == 1:
            wait("g", sem("tp2"), 8 * 16)
        else:
            wait("g", sem("fmm"), 8 * (op - 1))
        emit("g", lambda e, _op=op, _b=buf: e.dma_start(
            out=wf_db[_b][:, :].rearrange("p (hc o) -> p hc o", hc=32),
            in_=wffn_t[_op].rearrange("(hc p) o -> p hc o", p=128)).then_inc(sem("wfl").h, 16))
        wfl_vals.append(sem("wfl").bump(16))
        if op >= 2:
            wait("g", sem("fosc"), 8 * (op - 1))
        emit("g", lambda e, _op=op, _b=buf: e.dma_start(
            out=ocsl[_b][:, :], in_=ocsc_d[:, _op * 256:(_op + 1) * 256]).then_inc(sem("ocl").h, 16))
        ocl_vals.append(sem("ocl").bump(16))

    def emit_sign(op):
        wait("v", sem("wfl"), wfl_vals[op])
        emit("v", lambda e, _b=op % 2: e.tensor_scalar(
            wf_db[_b][:, :].bitcast(U16), wf_db[_b][:, :].bitcast(U16),
            int(0x8000), int(0x3F80), ALU.bitwise_and, ALU.bitwise_or))
        dinc("v", sem("sgn"))

    emit_wf_load(0)
    emit_wf_load(1)
    emit_sign(0)

    for op in range(16):
        buf = op % 2
        if op + 2 < 16:
            emit_wf_load(op + 2)
        if op + 1 < 16:
            emit_sign(op + 1)

        wait("t", sem("sgn"), op + 1)
        wait("t", sem("ocl"), ocl_vals[op])
        if op == 0:
            wait("t", sem("iev"), iev_all)
        for tc in range(NTC):
            bank_acquire(7)
            emit("t", lambda e, _tc=tc, _b=buf: e.matmul(
                bank[7][:, 0:256], rtT[0:8, _tc * 128:(_tc + 1) * 128],
                ocsl[_b][0:8, :], start=True, stop=True))
            foscv = dinc("t", sem("fosc"))
            b = (op * 8 + tc) % 7
            bank_acquire(b)
            for hc in range(NHC):
                emit("t", lambda e, _b=b, _hc=hc, _tc=tc, _wb=buf: e.matmul(
                    bank[_b][:, 0:256],
                    x1nT[:, _hc * 1024 + _tc * 128: _hc * 1024 + (_tc + 1) * 128],
                    wf_db[_wb][:, _hc * 256:(_hc + 1) * 256],
                    start=(_hc == 0), stop=(_hc == NHC - 1)))
            fv = dinc("t", sem("fmm"))

            wait("v", sem("fosc"), foscv)
            emit("v", lambda e: e.tensor_copy(osc_sb[:, :], bank[7][:, 0:256]))
            oscvv = dinc("v", sem("oscv"))
            bank_release(7, sem("oscv"), oscvv)
            wait("v", sem("fmm"), fv)
            emit("v", lambda e, _b=b, _tc=tc, _op=op: e.tensor_tensor(
                vffn[:, _tc * 4096 + _op * 256: _tc * 4096 + (_op + 1) * 256],
                bank[_b][:, 0:256], osc_sb[:, :], ALU.mult).then_inc(sem("fev").h, 1))
            fevv = sem("fev").bump(1)
            bank_release(b, sem("fev"), fevv)
            if op == 0 and tc == 0:
                wait("v", sem("bfl"), 16)
            vdrain()
            emit("v", lambda e, _tc=tc, _op=op: e.tensor_tensor(
                vffn[:, _tc * 4096 + _op * 256: _tc * 4096 + (_op + 1) * 256],
                vffn[:, _tc * 4096 + _op * 256: _tc * 4096 + (_op + 1) * 256],
                bffn_bc[:, _op * 256:(_op + 1) * 256], ALU.add))
            vdrain()
            emit("v", lambda e, _tc=tc, _op=op: e.tensor_scalar(
                scrap256a[:, :],
                vffn[:, _tc * 4096 + _op * 256: _tc * 4096 + (_op + 1) * 256],
                1.0, 0.0, ALU.mult, ALU.add,
                accum_out=fsum[:, _tc * 16 + _op:_tc * 16 + _op + 1]))
            emit("v", lambda e, _tc=tc, _op=op: e.tensor_tensor(
                scrap256b[:, :],
                vffn[:, _tc * 4096 + _op * 256: _tc * 4096 + (_op + 1) * 256],
                vffn[:, _tc * 4096 + _op * 256: _tc * 4096 + (_op + 1) * 256],
                ALU.mult))
            emit("v", lambda e, _tc=tc, _op=op: e.tensor_scalar(
                scrap256b[:, :], scrap256b[:, :], 1.0, 0.0, ALU.mult, ALU.add,
                accum_out=fsq[:, _tc * 16 + _op:_tc * 16 + _op + 1]))
            dinc("v", sem("fsumv"))

    fsum_all = sem("fsumv").n
    for tc in range(NTC):
        if tc >= 1:
            wait("s", sem("yst"), tc * 32)
        else:
            wait("s", sem("x1st"), 128)
            wait("s", sem("fsumv"), fsum_all)
        for hh in range(2):
            emit("s", lambda e, _t=tc, _h=hh: e.dma_start(
                out=x1_db[_h][:, :],
                in_=x1_d[_t * 128:(_t + 1) * 128, _h * 2048:(_h + 1) * 2048]).then_inc(sem("x1ld").h, 16))
            sem("x1ld").bump(16)
        x1lv = sem("x1ld").n
        wait("v", sem("fsumv"), fsum_all)
        wait("v", sem("btotv"), 1)
        emit("v", lambda e, _tc=tc: e.reduce_sum(stc(SUM), fsum[:, _tc * 16:(_tc + 1) * 16], axis=AX.X))
        vdrain()
        emit("v", lambda e: e.tensor_tensor(stc(SUM), stc(SUM), st[:, 24:25], ALU.add))
        emit("v", lambda e, _tc=tc: e.reduce_sum(stc(SQ), fsq[:, _tc * 16:(_tc + 1) * 16], axis=AX.X))
        emit_meanvar()
        srv = emit_rstd()
        wait("v", sem("srstd"), srv)
        emit("v", lambda e, _tc=tc: e.tensor_scalar(
            vffn[:, _tc * 4096:(_tc + 1) * 4096], vffn[:, _tc * 4096:(_tc + 1) * 4096],
            stc(M), stc(RSTD), ALU.subtract, ALU.mult))
        vdrain()
        wait("v", sem("x1ld"), x1lv)
        for hh in range(2):
            emit("v", lambda e, _t=tc, _h=hh: e.tensor_tensor(
                x1_db[_h][:, :], x1_db[_h][:, :],
                vffn[:, _t * 4096 + _h * 2048: _t * 4096 + (_h + 1) * 2048],
                ALU.add))
            dinc("v", sem("ytt"))
        wait("y", sem("ytt"), sem("ytt").n)
        for hh in range(2):
            emit("y", lambda e, _t=tc, _h=hh: e.dma_start(
                out=y_d[_t * 128:(_t + 1) * 128, _h * 2048:(_h + 1) * 2048],
                in_=x1_db[_h][:, :]).then_inc(sem("yst").h, 16))
            sem("yst").bump(16)

    wait("y", sem("yst"), sem("yst").n)
    if debug or light:
        wait("y", sem("dbg"), sem("dbg").n)

    mine = set()
    _my_prefixes = None

    with nc.Block() as block:
        @block.tensor
        def _(e):
            for fn in ops["t"]:
                fn(e)

        @block.vector
        def _(e):
            for fn in ops["v"]:
                fn(e)

        @block.scalar
        def _(e):
            for fn in ops["s"]:
                fn(e)

        @block.gpsimd
        def _(e):
            for fn in ops["g"]:
                fn(e)

        @block.sync
        def _(e):
            for fn in ops["y"]:
                fn(e)

    for a in nc.m.functions[0].allocations:
        if type(a).__name__ != "MemoryLocationSet" or not getattr(a, "memorylocations", None):
            continue
        for ml in a.memorylocations:
            if str(getattr(ml, "type", "")) != "SB" or not getattr(ml, "allocated", False):
                continue
            if ml.name == "DynamicDMAScratchLoc":
                continue
            base = ml.name.rsplit("_", 1)[0] if "_" in ml.name else ml.name
            if base in manual_names or ml.name in manual_names:
                continue
            end = ml.addr + int(ml.dims[1])
            assert ml.addr >= 16384 and end <= SBUF_BASE, (
                f"auto SBUF alloc {ml.name} [{ml.addr},{end}) overlaps manual region")
    return nc


def make_in_maps(inputs):
    x = np.asarray(inputs["L0_attn_x"])
    Wqkv = np.asarray(inputs["L0_attn_Wqkv"])
    bqkv = np.asarray(inputs["L0_attn_bqkv"])
    Wo = np.asarray(inputs["L0_attn_Wo"])
    bo = np.asarray(inputs["L0_attn_bo"])
    Wf = np.asarray(inputs["L0_ffn_weight"])
    bf = np.asarray(inputs["L0_ffn_bias"])
    gw = np.asarray(inputs["L0_ffn_gate_w"])
    ics = np.asarray(inputs["L0_ffn_in_channel_scale"])
    ocs = np.asarray(inputs["L0_ffn_out_channel_scale"])

    order = list(range(8, 16)) + list(range(16, 24)) + list(range(0, 8))
    wqkv_t = np.ascontiguousarray(
        np.stack([Wqkv[:, c * 128:(c + 1) * 128] for c in order], axis=0))
    bqkv_r = np.ascontiguousarray(
        np.stack([bqkv[c * 128:(c + 1) * 128] for c in order], axis=1))
    wffn_t = np.ascontiguousarray(
        np.stack([Wf[i * 256:(i + 1) * 256, :].T for i in range(16)], axis=0))
    shared = {
        "wqkv_t": wqkv_t,
        "bqkv_r": bqkv_r,
        "wo_r": np.ascontiguousarray(Wo),
        "bo_row": np.ascontiguousarray(bo[None, :]),
        "wffn_t": wffn_t,
        "bffn_bcast": np.ascontiguousarray(np.broadcast_to(bf[None, :], (128, H))),
        "gate_t": np.ascontiguousarray(gw.T),
        "icsc_d": np.ascontiguousarray(ics),
        "ocsc_d": np.ascontiguousarray(ocs),
    }
    in_maps = []
    for c in range(8):
        b, h = c // 2, c % 2
        m = dict(shared)
        m["x_tm"] = np.ascontiguousarray(x[b, h * T:(h + 1) * T, :])
        in_maps.append(m)
    return in_maps


def kernel(**inputs):
    nc = build_program(debug=False)
    in_maps = make_in_maps(inputs)
    res = run_bass_kernel_spmd(nc, in_maps, core_ids=list(range(8)))
    out = np.empty((4, 2048, H), dtype=np.float32)
    for c in range(8):
        b, h = c // 2, c % 2
        out[b, h * T:(h + 1) * T, :] = res.results[c]["y_d"]
    return out
